# version 08
# baseline (speedup 1.0000x reference)
import os
import numpy as np
import ml_dtypes
from scipy.special import erfinv

import concourse.bass as bass
import concourse.mybir as mybir
from concourse.bass_utils import run_bass_kernel_spmd

F32 = mybir.dt.float32
BF16 = mybir.dt.bfloat16
AF = mybir.ActivationFunctionType
ALU = mybir.AluOpType
AXX = mybir.AxisListType.X

NCORES = 8
D = 2048
KD = D // 128
I_DIM = 1408
KI = I_DIM // 128
SI = 2816
KSI = SI // 128
E = 16
TOKS = 8192
SCALE = 1.0 / float(np.sqrt(2048.0))


def _threefry2x32(k0, k1, x0, x1):
    M = np.uint64(0xFFFFFFFF)
    def rotl(v, r):
        return ((v << np.uint64(r)) | (v >> np.uint64(32 - r))) & M
    ks0 = np.uint64(k0); ks1 = np.uint64(k1)
    ks2 = (ks0 ^ ks1 ^ np.uint64(0x1BD11BDA)) & M
    x0 = x0.astype(np.uint64); x1 = x1.astype(np.uint64)
    rot = [(13, 15, 26, 6), (17, 29, 16, 24)]
    x0 = (x0 + ks0) & M; x1 = (x1 + ks1) & M
    inject = [(ks1, ks2, 1), (ks2, ks0, 2), (ks0, ks1, 3), (ks1, ks2, 4), (ks2, ks0, 5)]
    for blk in range(5):
        for r in rot[blk % 2]:
            x0 = (x0 + x1) & M
            x1 = rotl(x1, r)
            x1 = x1 ^ x0
        a, b, c = inject[blk]
        x0 = (x0 + a) & M
        x1 = (x1 + b + np.uint64(c)) & M
    return x0.astype(np.uint32), x1.astype(np.uint32)

def _noise(fp64=False):
    n = TOKS * E
    idx = np.arange(n, dtype=np.uint64)
    hi = (idx >> np.uint64(32)).astype(np.uint32)
    lo = (idx & np.uint64(0xFFFFFFFF)).astype(np.uint32)
    b1, b2 = _threefry2x32(0, 42, hi, lo)
    if not fp64:
        bits = b1 ^ b2
        f = ((bits >> np.uint32(9)) | np.uint32(0x3F800000)).view(np.float32) - np.float32(1.0)
        lo32 = np.float32(np.nextafter(np.float32(-1), np.float32(0)))
        u = np.maximum(lo32, (f * (np.float32(1.0) - lo32) + lo32).astype(np.float32))
        nrm = (np.float32(np.sqrt(2.0)) * erfinv(u.astype(np.float64))).astype(np.float32)
    else:
        bits = (b1.astype(np.uint64) << np.uint64(32)) | b2.astype(np.uint64)
        f = ((bits >> np.uint64(12)) | np.uint64(0x3FF0000000000000)).view(np.float64) - 1.0
        lo64 = np.nextafter(-1.0, 0.0)
        u = np.maximum(lo64, f * (1.0 - lo64) + lo64)
        nrm = (np.sqrt(2.0) * erfinv(u)).astype(np.float32)
    return nrm.reshape(TOKS, E)

def _routing(fp64=False):
    noise = _noise(fp64)
    s = (1.0 / (1.0 + np.exp(-noise.astype(np.float64)))).astype(np.float32)
    ids = np.argsort(-s.astype(np.float64), axis=-1, kind="stable")[:, :2]
    w = np.take_along_axis(s, ids, axis=-1).astype(np.float32)
    w = (w / (w.sum(-1, keepdims=True) + np.float32(1e-8))).astype(np.float32)
    return ids.astype(np.int64), w


class Layout:
    pass

def _build_layout(ids, w):
    L = Layout()
    a = np.minimum(ids[:, 0], ids[:, 1])
    b = np.maximum(ids[:, 0], ids[:, 1])
    pk = a * E + b
    tok_by_pair = {}
    for t in np.argsort(pk, kind="stable"):
        tok_by_pair.setdefault(int(pk[t]), []).append(int(t))
    pairs = []
    for key in sorted(tok_by_pair):
        toks = tok_by_pair[key]
        q = (len(toks) + NCORES - 1) // NCORES
        pairs.append((key // E, key % E, q, toks))
    L.pairs = pairs
    S = sum(p[2] for p in pairs)
    L.S = S
    L.S_tot = ((S + 127) // 128) * 128
    L.NB = L.S_tot // 128
    off = 0
    L.pair_off = []
    for (_, _, q, _) in pairs:
        L.pair_off.append(off)
        off += q
    L.exp_runs = []
    L.exp_off = []
    L.n_e = []
    soff = 0
    for e in range(E):
        runs = []
        ne = 0
        for (pa, pb_, q, _), poff in zip(pairs, L.pair_off):
            if pa == e or pb_ == e:
                runs.append((ne, poff, q))
                ne += q
        L.exp_runs.append(runs)
        L.exp_off.append(soff)
        L.n_e.append(ne)
        soff += ne
    L.Ltot = soff
    L.LB = (soff + 127) // 128
    L.perm = np.full((NCORES, L.S_tot), -1, dtype=np.int64)
    L.maskv = np.zeros((NCORES, L.S_tot), dtype=np.float32)
    for (pa, pb_, q, toks), poff in zip(pairs, L.pair_off):
        m = len(toks)
        base, rem = divmod(m, NCORES)
        start = 0
        for c in range(NCORES):
            cnt = base + (1 if c < rem else 0)
            for i in range(cnt):
                L.perm[c, poff + i] = toks[start + i]
                L.maskv[c, poff + i] = 1.0
            start += cnt
    L.wslot = np.zeros((NCORES, L.LB * 128), dtype=np.float32)
    for e in range(E):
        so = L.exp_off[e]
        for (ro, poff, q) in L.exp_runs[e]:
            for i in range(q):
                for c in range(NCORES):
                    t = L.perm[c, poff + i]
                    if t >= 0:
                        if ids[t, 0] == e:
                            L.wslot[c, so + ro + i] = w[t, 0]
                        else:
                            L.wslot[c, so + ro + i] = w[t, 1]
    return L


def _segments(dst0, src0, length, blk=128):
    out = []
    done = 0
    while done < length:
        d = dst0 + done
        s = src0 + done
        step = min(length - done, blk - d % blk, blk - s % blk)
        out.append((d // blk, d % blk, s // blk, s % blk, step))
        done += step
    return out



def build_program(L):
    nc = bass.Bass("TRN2", target_bir_lowering=False)
    S_tot, NB, LB = L.S_tot, L.NB, L.LB
    NF = (S_tot + 511) // 512
    ftiles = [(f * 512, min(512, S_tot - f * 512)) for f in range(NF)]
    EXW = ((max(L.n_e[2 * o] + L.n_e[2 * o + 1] for o in range(E // 2)) + 127) // 128) * 128
    NEMAX = max(L.n_e)

    xin = nc.dram_tensor("xin", [S_tot, D], F32, kind="ExternalInput").ap()
    wgur = nc.dram_tensor("wgur", [E, KI, 128, 2 * D], F32, kind="ExternalInput").ap()
    wdr = nc.dram_tensor("wdr", [E, KI, 128, D], F32, kind="ExternalInput").ap()
    sgur = nc.dram_tensor("sgur", [KSI, 128, 2 * D], F32, kind="ExternalInput").ap()
    sdr = nc.dram_tensor("sdr", [KSI, 128, D], F32, kind="ExternalInput").ap()
    wqr = nc.dram_tensor("wqr", [KD, 128, D], F32, kind="ExternalInput").ap()
    wkr = nc.dram_tensor("wkr", [KD, 128, D], F32, kind="ExternalInput").ap()
    wvr = nc.dram_tensor("wvr", [KD, 128, D], F32, kind="ExternalInput").ap()
    bqr = nc.dram_tensor("bqr", [KD, 128, 1], F32, kind="ExternalInput").ap()
    bkr = nc.dram_tensor("bkr", [KD, 128, 1], F32, kind="ExternalInput").ap()
    bvr = nc.dram_tensor("bvr", [KD, 128, 1], F32, kind="ExternalInput").ap()
    wslr = nc.dram_tensor("wslr", [E, 2, 128, 1], F32, kind="ExternalInput").ap()
    mkr = nc.dram_tensor("mkr", [NB, 128, 1], F32, kind="ExternalInput").ap()
    idnr = nc.dram_tensor("idnr", [128, 128], BF16, kind="ExternalInput").ap()
    xout = nc.dram_tensor("xout", [S_tot, D], F32, kind="ExternalOutput").ap()

    qscr = nc.dram_tensor("qscr", [KD, NB, 128, 128], BF16)
    vscr = nc.dram_tensor("vscr", [KD, 128, S_tot], BF16)
    cc_s_in = nc.dram_tensor("cc_s_in", [D, D], BF16)
    cc_s_out = nc.dram_tensor("cc_s_out", [D // NCORES, D], BF16)
    cc_p_in = nc.dram_tensor("cc_p_in", [KD, 128, D // NCORES], BF16)
    cc_p_out = nc.dram_tensor("cc_p_out", [NCORES, KD, 128, D // NCORES], BF16)
    RG = [list(range(NCORES))]

    import contextlib
    stck = contextlib.ExitStack()
    def sbuf(name, shape, dt):
        return stck.enter_context(nc.sbuf_tensor(name, shape, dt))
    def psum(name, shape, dt=F32):
        return stck.enter_context(nc.psum_tensor(name, shape, dt))

    x = [sbuf(f"x{b}", [128, D], F32) for b in range(NB)]
    xn = [sbuf(f"xn{k}", [128, S_tot], BF16) for k in range(KD)]
    tm9 = [sbuf(f"tm{b}", [128, D], BF16) for b in range(NB)]
    dtm = [sbuf(f"dtm{j}", [128, D], BF16) for j in range(2)]
    NSH = 1
    hsh = [sbuf(f"hsh{j}", [128, S_tot], BF16) for j in range(NSH)]
    ex = [sbuf(f"ex{k}", [128, EXW], BF16) for k in range(KD)]
    he = [sbuf(f"he{k}", [128, NEMAX], BF16) for k in range(KI)]
    wA = [sbuf(f"wA{j}", [128, 2 * D], BF16) for j in range(2)]
    wC = [sbuf(f"wC{j}", [128, D], BF16) for j in range(2)]
    pex = sbuf("pex", [128, D], F32)
    sctm = sbuf("sctm", [128, D], BF16)
    wcol = [sbuf(f"wc{g}", [128, 1], F32) for g in range(2)]
    mcol = [sbuf(f"mc{b}", [128, 1], F32) for b in range(NB)]
    bqc = [sbuf(f"bqc{k}", [128, 1], F32) for k in range(KD)]
    bkc = [sbuf(f"bkc{k}", [128, 1], F32) for k in range(KD)]
    bvc = [sbuf(f"bvc{k}", [128, 1], F32) for k in range(KD)]
    idt = sbuf("idt", [128, 128], BF16)
    st6 = sbuf("st6", [128, 24], F32)
    red4 = sbuf("red4", [128, 4], F32)
    mv2 = sbuf("mv2", [128, 2], F32)
    rstd = sbuf("rstd", [128, 1], F32)
    nmr = sbuf("nmr", [128, 1], F32)
    epsb = sbuf("epsb", [128, 1], F32)
    mx1 = sbuf("mx1", [128, 1], F32)
    sm1 = sbuf("sm1", [128, 1], F32)

    pb = [psum(f"pb{i}", [128, 512]) for i in range(8)]

    sems, ctr, lastw = {}, {}, {}
    for name in ("sq", "sg2", "smm", "sve", "sse", "scc", "sgm"):
        sems[name] = stck.enter_context(nc.semaphore(name))
        ctr[name] = 0

    TEN, VEC, SCA, GPS, SYN = nc.tensor, nc.vector, nc.scalar, nc.gpsimd, nc.sync
    ENGSEM = {id(TEN): "smm", id(VEC): "sve", id(SCA): "sse"}

    def inc(instr, name, amt):
        instr.then_inc(sems[name], amt)
        ctr[name] += amt
        return (name, ctr[name])

    def need(eng, *tokens):
        for t in tokens:
            if t is None:
                continue
            name, val = t
            key = (id(eng), name)
            if lastw.get(key, -1) >= val:
                continue
            eng.wait_ge(sems[name], val)
            lastw[key] = val

    def dma(eng, qsem, out, in_, accum=False, deps=()):
        need(eng, *deps)
        if accum:
            ins = eng.dma_start(out=out, in_=in_, accum_op=ALU.add)
        else:
            ins = eng.dma_start(out=out, in_=in_)
        return inc(ins, qsem, 16)

    def op(eng, method, *args, deps=(), **kw):
        need(eng, *deps)
        ins = getattr(eng, method)(*args, **kw)
        return inc(ins, ENGSEM[id(eng)], 1)

    bank_free = {i: None for i in range(8)}
    buf_free = {}

    def mm(bi, out_ap, lhsT, rhs, start, stop, deps=()):
        d = deps + ((bank_free[bi],) if start else ())
        return op(TEN, "matmul", out_ap, lhsT, rhs, start=start, stop=stop, deps=d)

    def transpose128(bi, src_ap, deps=()):
        return mm(bi, pb[bi][:, 0:128], src_ap, idt[:, :], True, True,
                  deps=deps + (tok_id,))

    tok_x = [dma(SYN, "sq", x[b][:, :], xin[b * 128:(b + 1) * 128, :]) for b in range(NB)]

    tok_mc = [dma(SYN, "sq", mcol[b][:, :], mkr[b]) for b in range(NB)]
    tok_bq = [dma(SYN, "sq", bqc[k][:, :], bqr[k]) for k in range(KD)]
    tok_bk = [dma(SYN, "sq", bkc[k][:, :], bkr[k]) for k in range(KD)]
    tok_bv = [dma(SYN, "sq", bvc[k][:, :], bvr[k]) for k in range(KD)]
    tok_id = dma(SYN, "sq", idt[:, :], idnr[:, :])
    tok_eps = inc(GPS.memset(epsb[:, :], 1e-5), "sgm", 1)

    state = {"xtok": list(tok_x)}

    def ln_to_fm(masked, keep_tm):
        toks = []
        for b in range(NB):
            t1 = None
            for c4 in range(4):
                t1 = op(VEC, "bn_stats", st6[:, c4 * 6:(c4 + 1) * 6],
                        x[b][:, c4 * 512:(c4 + 1) * 512], deps=(state["xtok"][b],))
            t2 = op(VEC, "bn_aggr", mv2[:, :], st6[:, :], deps=(t1,))
            t3 = op(SCA, "activation", rstd[:, :], mv2[:, 1:2], AF.Sqrt, bias=epsb[:, :],
                    deps=(t2, tok_eps))
            t4 = op(VEC, "reciprocal", rstd[:, :], rstd[:, :], deps=(t3,))
            if masked:
                t4 = op(VEC, "tensor_mul", rstd[:, :], rstd[:, :], mcol[b][:, :],
                        deps=(t4, tok_mc[b]))
            t5 = None
            for c4 in range(4):
                sl = slice(c4 * 512, (c4 + 1) * 512)
                t5 = op(VEC, "tensor_scalar", sctm[:, sl], x[b][:, sl], mv2[:, 0:1],
                        rstd[:, :], ALU.subtract, ALU.mult, deps=(t4,) + tuple(
                            t for t in (buf_free.get("sctm"),) if t))
                if keep_tm:
                    toks.append(op(VEC, "tensor_scalar", tm9[b][:, sl], x[b][:, sl],
                                   mv2[:, 0:1], rstd[:, :], ALU.subtract, ALU.mult,
                                   deps=(t4,)))
            for k in range(KD):
                bi = 6 + (k % 2)
                tt = transpose128(bi, sctm[:, k * 128:(k + 1) * 128], deps=(t5,))
                tc = op(VEC, "tensor_copy", xn[k][:, b * 128:(b + 1) * 128],
                        pb[bi][:, 0:128], deps=(tt,))
                bank_free[bi] = tc
                toks.append(tc)
            buf_free["sctm"] = toks[-1]
        return toks[-1]

    def ffn_pass():
        xn_ready = ln_to_fm(masked=False, keep_tm=True)
        tm_ready = xn_ready

        last_sh_add = None
        for g0 in range(0, KSI, NSH):
            grp = list(range(g0, min(g0 + NSH, KSI)))
            h_last = None
            for j, si in enumerate(grp):
                pA = wA[j % 2]
                depA = tuple(t for t in (buf_free.get(f"wA{j % 2}"),) if t)
                dma(GPS, "sg2", pA[:, 0:D], sgur[si][:, 0:D], deps=depA)
                tA = dma(GPS, "sg2", pA[:, D:2 * D], sgur[si][:, D:2 * D])
                for (f0, flen) in ftiles:
                    g_mm = None
                    for k in range(KD):
                        g_mm = mm(0, pb[0][:, :flen], pA[:, k * 128:(k + 1) * 128],
                                  xn[k][:, f0:f0 + flen], k == 0, k == KD - 1,
                                  deps=(tA, xn_ready))
                    u_mm = None
                    for k in range(KD):
                        u_mm = mm(1, pb[1][:, :flen], pA[:, D + k * 128:D + (k + 1) * 128],
                                  xn[k][:, f0:f0 + flen], k == 0, k == KD - 1, deps=(tA,))
                    ts = op(SCA, "activation", pex[:, 0:flen], pb[0][:, :flen], AF.Silu,
                            deps=(g_mm,) + tuple(t for t in (buf_free.get("sil"),) if t))
                    th = op(VEC, "tensor_mul", hsh[j][:, f0:f0 + flen], pex[:, 0:flen],
                            pb[1][:, :flen], deps=(ts, u_mm))
                    buf_free["sil"] = th
                    bank_free[0] = ts
                    bank_free[1] = th
                    h_last = th
                buf_free[f"wA{j % 2}"] = h_last
            sd_tiles = []
            for j, si in enumerate(grp):
                buf = wC[j % 2]
                td = dma(GPS, "sg2", buf[:, :], sdr[si],
                         deps=tuple(t for t in (buf_free.get(f"wC{j % 2}"),) if t))
                th = None
                for c4 in range(4):
                    th = op(SCA, "mul", buf[:, c4 * 512:(c4 + 1) * 512],
                            buf[:, c4 * 512:(c4 + 1) * 512], 0.5, deps=(td,))
                sd_tiles.append((buf, th))
            dn_last = None
            for b in range(NB):
                for dq in range(4):
                    bi = 2 + dq
                    d_mm = None
                    for j in range(len(grp)):
                        buf, th = sd_tiles[j]
                        d_mm = mm(bi, pb[bi][:, :], hsh[j][:, b * 128:(b + 1) * 128],
                                  buf[:, dq * 512:(dq + 1) * 512], j == 0, j == len(grp) - 1,
                                  deps=(th, h_last))
                    ta = op(VEC, "tensor_add", x[b][:, dq * 512:(dq + 1) * 512],
                            x[b][:, dq * 512:(dq + 1) * 512], pb[bi][:, :],
                            deps=(d_mm, state["xtok"][b]))
                    bank_free[bi] = ta
                    state["xtok"][b] = ta
                    dn_last = ta
            buf_free["wC0"] = dn_last
            buf_free["wC1"] = dn_last
            last_sh_add = dn_last

        prev_scat = None
        for oc in range(E // 2):
            e0, e1 = 2 * oc, 2 * oc + 1
            oct_off = L.exp_off[e0]
            oct_len = L.n_e[e0] + L.n_e[e1]
            oct_blocks = (oct_len + 127) // 128
            pk_last = None
            for db in range(oct_blocks):
                rows = min(128, oct_len - db * 128)
                segs = []
                mz = None
                if rows < 128:
                    mz = op(VEC, "memset", sctm[:, :], 0.0,
                            deps=tuple(t for t in (buf_free.get("sctm"),) if t))
                    segs.append(mz)
                for e in (e0, e1):
                    lo = L.exp_off[e] - oct_off
                    for (ro, poff, q) in L.exp_runs[e]:
                        for (d_b, d_r, s_b, s_r, ln_) in _segments(lo + ro, poff, q):
                            if d_b != db:
                                continue
                            segs.append(dma(GPS, "sg2", sctm[d_r:d_r + ln_, :],
                                            tm9[s_b][s_r:s_r + ln_, :],
                                            deps=(tm_ready,) + tuple(
                                                t for t in (mz, buf_free.get("sctm")) if t)))
                tpk = None
                for k in range(KD):
                    bi = 6 + (k % 2)
                    tt = transpose128(bi, sctm[:, k * 128:(k + 1) * 128], deps=tuple(segs))
                    tpk = op(VEC, "tensor_copy", ex[k][:, db * 128:(db + 1) * 128],
                             pb[bi][:, 0:128], deps=(tt,) + tuple(
                                 t for t in (buf_free.get("ex"),) if t))
                    bank_free[bi] = tpk
                buf_free["sctm"] = tpk
                pk_last = tpk
            for e in (e0, e1):
                lo = L.exp_off[e] - oct_off
                ne = L.n_e[e]
                h_tok = []
                for i in range(KI):
                    pA = wA[i % 2]
                    depA = (h_tok[i - 2],) if i >= 2 else tuple(
                        t for t in (buf_free.get(f"wA{i % 2}"),) if t)
                    dma(GPS, "sg2", pA[:, 0:D], wgur[e, i][:, 0:D], deps=depA)
                    tA = dma(GPS, "sg2", pA[:, D:2 * D], wgur[e, i][:, D:2 * D])
                    g_mm = None
                    for k in range(KD):
                        g_mm = mm(0, pb[0][:, :ne], pA[:, k * 128:(k + 1) * 128],
                                  ex[k][:, lo:lo + ne], k == 0, k == KD - 1,
                                  deps=(tA, pk_last))
                    u_mm = None
                    for k in range(KD):
                        u_mm = mm(1, pb[1][:, :ne], pA[:, D + k * 128:D + (k + 1) * 128],
                                  ex[k][:, lo:lo + ne], k == 0, k == KD - 1, deps=(tA,))
                    ts = op(SCA, "activation", pex[:, 0:ne], pb[0][:, :ne], AF.Silu,
                            deps=(g_mm,) + tuple(t for t in (buf_free.get("sil"),) if t))
                    th = op(VEC, "tensor_mul", he[i][:, :ne], pex[:, 0:ne], pb[1][:, :ne],
                            deps=(ts, u_mm))
                    buf_free["sil"] = th
                    bank_free[0] = ts
                    bank_free[1] = th
                    h_tok.append(th)
                buf_free["wA0"] = h_tok[-1]; buf_free["wA1"] = h_tok[-1]
                grp_rows = [(g * 128, min(128, ne - g * 128))
                            for g in range((ne + 127) // 128)]
                last_mm = [None] * 8
                dn_t = []
                for i in range(KI):
                    buf = wC[i % 2]
                    depW = (last_mm[0],) if i >= 2 else tuple(
                        t for t in (buf_free.get(f"wC{i % 2}"),) if t)
                    dn_t.append(dma(GPS, "sg2", buf[:, :], wdr[e, i], deps=depW))
                    for gi, (g0, rows) in enumerate(grp_rows):
                        for dq in range(4):
                            bi = gi * 4 + dq
                            last_mm[bi] = mm(bi, pb[bi][:rows, :],
                                             he[i][:, g0:g0 + rows],
                                             buf[:, dq * 512:(dq + 1) * 512],
                                             i == 0, i == KI - 1,
                                             deps=(dn_t[i], h_tok[i]))
                buf_free["wC0"] = last_mm[0]; buf_free["wC1"] = last_mm[0]
                ev = []
                w_toks = []
                for gi in range(len(grp_rows)):
                    tw = dma(SYN, "sq", wcol[gi][:, :], wslr[e, gi],
                             deps=tuple(t for t in (buf_free.get(f"wcol{gi}"),) if t))
                    tw = op(SCA, "mul", wcol[gi][:, :], wcol[gi][:, :], 0.5, deps=(tw,))
                    w_toks.append(tw)
                for gi, (g0, rows) in enumerate(grp_rows):
                    for dq in range(4):
                        bi = gi * 4 + dq
                        tv = op(VEC, "tensor_scalar_mul",
                                dtm[gi][0:rows, dq * 512:(dq + 1) * 512],
                                pb[bi][0:rows, :], wcol[gi][0:rows, :],
                                deps=(last_mm[bi], w_toks[gi]) + tuple(
                                    t for t in (buf_free.get(f"dtm{gi}"),) if t))
                        ev.append(tv)
                        bank_free[bi] = tv
                    buf_free[f"wcol{gi}"] = ev[-1]
                ev_tok = ev[-1]
                sc_deps = (ev_tok, last_sh_add)
                if prev_scat is not None:
                    sc_deps = sc_deps + (prev_scat,)
                for gi, (g0, rows) in enumerate(grp_rows):
                    pass
                for (ro, poff, q) in L.exp_runs[e]:
                    done = 0
                    while done < q:
                        sl = lo * 0 + ro + done
                        gi, sr = sl // 128, sl % 128
                        dtok = poff + done
                        db, dr = dtok // 128, dtok % 128
                        step = min(q - done, 128 - sr, 128 - dr)
                        td = dma(GPS, "sg2", x[db][dr:dr + step, :],
                                 dtm[gi][sr:sr + step, :], accum=True, deps=sc_deps)
                        state["xtok"][db] = td
                        done += step
                prev_scat = ("sg2", ctr["sg2"])
                buf_free["dtm0"] = prev_scat
                buf_free["dtm1"] = prev_scat
        buf_free["ex"] = prev_scat

    def attention():
        xn_ready = ln_to_fm(masked=True, keep_tm=False)

        def proj(wsrc, bias_cols, bias_toks, sink):
            for mc in range(KD):
                buf = wA[mc % 2]
                depW = tuple(t for t in (buf_free.get(f"wA{mc % 2}"),) if t)
                tW = dma(GPS, "sg2", buf[:, 0:D], wsrc[mc], deps=depW)
                act = None
                for (f0, flen) in ftiles:
                    p_mm = None
                    for k in range(KD):
                        p_mm = mm(0, pb[0][:, :flen], buf[:, k * 128:(k + 1) * 128],
                                  xn[k][:, f0:f0 + flen], k == 0, k == KD - 1,
                                  deps=(tW, xn_ready))
                    act = op(VEC, "tensor_scalar_add", sctm[:, f0:f0 + flen],
                             pb[0][:, :flen], bias_cols[mc][:, :],
                             deps=(p_mm, bias_toks[mc]) + tuple(
                                 t for t in (buf_free.get("sctm"),) if t))
                    bank_free[0] = act
                last = sink(mc, act)
                buf_free["sctm"] = last
                buf_free[f"wA{mc % 2}"] = last

        def q_sink(mc, t_act):
            out = None
            for b in range(NB):
                bi = 6 + (b % 2)
                tt = transpose128(bi, sctm[:, b * 128:(b + 1) * 128], deps=(t_act,))
                tmk = op(VEC, "tensor_scalar_mul", he[0][:, 0:128], pb[bi][:, 0:128],
                         mcol[b][:, :], deps=(tt, tok_mc[b]) + tuple(
                             t for t in (buf_free.get("spb"),) if t))
                bank_free[bi] = tmk
                out = dma(SYN, "sq", qscr[mc, b], he[0][:, 0:128], deps=(tmk,))
                buf_free["spb"] = out
            return out
        proj(wqr, bqc, tok_bq, q_sink)

        def k_sink(mc, t_act):
            out = None
            for b in range(NB):
                bi = 6 + (b % 2)
                tt = transpose128(bi, sctm[:, b * 128:(b + 1) * 128], deps=(t_act,))
                out = op(VEC, "tensor_scalar_mul", tm9[b][:, mc * 128:(mc + 1) * 128],
                         pb[bi][:, 0:128], mcol[b][:, :], deps=(tt, tok_mc[b]))
                bank_free[bi] = out
            return out
        proj(wkr, bkc, tok_bk, k_sink)

        def v_sink(mc, t_act):
            return dma(SYN, "sq", vscr[mc], sctm[:, 0:S_tot], deps=(t_act,))
        proj(wvr, bvc, tok_bv, v_sink)

        k_done = ("sve", ctr["sve"])
        vq_done = ("sq", ctr["sq"])

        ex_mm = {}
        stgs = [dtm[0], dtm[1]]
        for lc in range(KD):
            par = lc % 2
            q_toks = []
            for b in range(NB):
                q_toks.append(dma(GPS, "sg2", ex[b][:, par * 128:par * 128 + 128],
                                  qscr[lc, b],
                                  deps=((ex_mm.get(par),) if ex_mm.get(par) else ())))
            sbuf_st = stgs[par]
            ev_t = None
            for mq in range(4):
                bi = 2 + mq
                s_mm = None
                for b in range(NB):
                    s_mm = mm(bi, pb[bi][:, :], ex[b][:, par * 128:par * 128 + 128],
                              tm9[b][:, mq * 512:(mq + 1) * 512], b == 0, b == NB - 1,
                              deps=(q_toks[b], k_done))
                ev_t = op(VEC, "tensor_copy", sbuf_st[:, mq * 512:(mq + 1) * 512],
                          pb[bi][:, :],
                          deps=(s_mm,) + tuple(
                              t for t in (buf_free.get(f"dtm{par}"),) if t))
                bank_free[bi] = ev_t
            ex_mm[par] = ("smm", ctr["smm"])
            tsd = dma(SYN, "sq", cc_s_in[lc * 128:(lc + 1) * 128, :], sbuf_st[:, :],
                      deps=(ev_t,))
            buf_free[f"dtm{par}"] = tsd
        sc_done = ("sq", ctr["sq"])

        need(GPS, sc_done)
        tcc = inc(GPS.collective_compute("ReduceScatter", ALU.add, replica_groups=RG,
                                         ins=[cc_s_in.ap().opt()],
                                         outs=[cc_s_out.ap().opt()]), "scc", 1)

        ag_last = None
        for sb in range(2):
            tl = dma(GPS, "sg2", pex[:, :], cc_s_out[sb * 128:(sb + 1) * 128, :],
                     deps=(tcc,) + tuple(t for t in (buf_free.get("pex"),) if t))
            tm_ = None
            for c4 in range(4):
                tm_ = op(VEC, "reduce_max", red4[:, c4:c4 + 1],
                         pex[:, c4 * 512:(c4 + 1) * 512], axis=AXX, deps=(tl,))
            tm_ = op(VEC, "reduce_max", mx1[:, :], red4[:, :], axis=AXX, deps=(tm_,))
            tn = op(SCA, "mul", nmr[:, :], mx1[:, :], -SCALE, deps=(tm_,))
            te = None
            for c4 in range(4):
                sl = slice(c4 * 512, (c4 + 1) * 512)
                te = op(SCA, "activation", pex[:, sl], pex[:, sl], AF.Exp, bias=nmr[:, :],
                        scale=SCALE, deps=(tn,))
            tsum = None
            for c4 in range(4):
                tsum = op(VEC, "reduce_sum", red4[:, c4:c4 + 1],
                          pex[:, c4 * 512:(c4 + 1) * 512], axis=AXX, deps=(te,))
            tsum = op(VEC, "reduce_sum", sm1[:, :], red4[:, :], axis=AXX, deps=(tsum,))
            tre = op(VEC, "reciprocal", sm1[:, :], sm1[:, :], deps=(tsum,))
            tp = None
            for c4 in range(4):
                sl = slice(c4 * 512, (c4 + 1) * 512)
                tp = op(VEC, "tensor_scalar_mul", sctm[:, sl], pex[:, sl], sm1[:, :],
                        deps=(tre,) + tuple(t for t in (buf_free.get("sctm"),) if t))
            buf_free["pex"] = tp
            for mc in range(KD):
                bi = 6 + (mc % 2)
                tt = transpose128(bi, sctm[:, mc * 128:(mc + 1) * 128], deps=(tp,))
                tc = op(VEC, "tensor_copy", he[0][:, 0:128], pb[bi][:, 0:128],
                        deps=(tt,) + tuple(t for t in (buf_free.get("spb"),) if t))
                bank_free[bi] = tc
                ag_last = dma(SYN, "sq", cc_p_in[mc, :, sb * 128:(sb + 1) * 128], he[0][:, 0:128],
                              deps=(tc,))
                buf_free["spb"] = ag_last
            buf_free["sctm"] = ag_last
        need(GPS, ("sq", ctr["sq"]))
        tag = inc(GPS.collective_compute("AllGather", ALU.bypass, replica_groups=RG,
                                         ins=[cc_p_in.ap().opt()],
                                         outs=[cc_p_out.ap().opt()]), "scc", 1)

        ex_mm2 = {}
        wb_user = {0: buf_free.get("wA0"), 1: buf_free.get("wA1")}
        for cb in range(NB):
            par = cb % 2
            v_toks = []
            for k in range(KD):
                v_toks.append(dma(GPS, "sg2", ex[k][:, par * 128:par * 128 + 128],
                                  vscr[k, :, cb * 128:(cb + 1) * 128],
                                  deps=((ex_mm2.get(par),) if ex_mm2.get(par) else ())))
            mm_lq = {}
            for k in range(KD):
                buf = wA[k % 2]
                depB = (wb_user.get(k % 2),) if wb_user.get(k % 2) else ()
                pt_t = None
                for r in range(NCORES):
                    pt_t = dma(GPS, "sg2", buf[:, r * 256:(r + 1) * 256], cc_p_out[r, k],
                               deps=(tag,) + (depB if r == 0 else ()))
                for lq in range(4):
                    bi = 2 + lq
                    mm_lq[lq] = mm(bi, pb[bi][:, :], ex[k][:, par * 128:par * 128 + 128],
                                   buf[:, lq * 512:(lq + 1) * 512], k == 0, k == KD - 1,
                                   deps=(v_toks[k], pt_t))
                wb_user[k % 2] = mm_lq[3]
            ex_mm2[par] = ("smm", ctr["smm"])
            for lq in range(4):
                bi = 2 + lq
                ta = op(VEC, "tensor_add", x[cb][:, lq * 512:(lq + 1) * 512],
                        x[cb][:, lq * 512:(lq + 1) * 512], pb[bi][:, :],
                        deps=(mm_lq[lq], state["xtok"][cb]))
                bank_free[bi] = ta
                state["xtok"][cb] = ta
            buf_free["wA0"] = wb_user[0]
            buf_free["wA1"] = wb_user[1]

    ffn_pass()
    attention()
    ffn_pass()

    for b in range(NB):
        dma(SYN, "sq", xout[b * 128:(b + 1) * 128, :], x[b][:, :],
            deps=(state["xtok"][b],))
    need(GPS, ("sq", ctr["sq"]))

    stck.close()
    return nc



def kernel(L0_attn_x, L0_attn_Wq, L0_attn_bq, L0_attn_Wk, L0_attn_bk, L0_attn_Wv,
           L0_attn_bv, L0_ffn_router_w, L0_ffn_router_b, L0_ffn_w_gate, L0_ffn_w_up,
           L0_ffn_w_down, L0_ffn_shared_gate, L0_ffn_shared_up, L0_ffn_shared_down):
    fp64 = os.environ.get("BASS_SIM_FP64", "") not in ("", "0")
    ids, w = _routing(fp64)
    L = _build_layout(ids, w)
    nc = build_program(L)

    xfull = np.ascontiguousarray(np.asarray(L0_attn_x, dtype=np.float32).reshape(TOKS, D))

    def relay_lhs(m):
        Dm, N = m.shape
        r = m.reshape(Dm // 128, 128, N // 128, 128).transpose(2, 1, 0, 3)
        return np.ascontiguousarray(r.reshape(N // 128, 128, Dm))

    wq_r = relay_lhs(np.asarray(L0_attn_Wq, np.float32).T)
    wk_r = relay_lhs(np.asarray(L0_attn_Wk, np.float32).T)
    wv_r = relay_lhs(np.asarray(L0_attn_Wv, np.float32).T)
    wg_r = np.stack([relay_lhs(np.asarray(L0_ffn_w_gate[e], np.float32)) for e in range(E)])
    wu_r = np.stack([relay_lhs(np.asarray(L0_ffn_w_up[e], np.float32)) for e in range(E)])
    wgu_r = np.ascontiguousarray(np.concatenate([wg_r, wu_r], axis=-1))
    wd_r = np.stack([np.ascontiguousarray(np.asarray(L0_ffn_w_down[e], np.float32)
                                          .reshape(KI, 128, D)) for e in range(E)])
    sg_r = relay_lhs(np.asarray(L0_ffn_shared_gate, np.float32))
    su_r = relay_lhs(np.asarray(L0_ffn_shared_up, np.float32))
    sgu_r = np.ascontiguousarray(np.concatenate([sg_r, su_r], axis=-1))
    sd_r = np.ascontiguousarray(np.asarray(L0_ffn_shared_down, np.float32).reshape(KSI, 128, D))
    bq_r = np.asarray(L0_attn_bq, np.float32).reshape(KD, 128, 1)
    bk_r = np.asarray(L0_attn_bk, np.float32).reshape(KD, 128, 1)
    bv_r = np.asarray(L0_attn_bv, np.float32).reshape(KD, 128, 1)
    ident = np.eye(128, dtype=ml_dtypes.bfloat16)

    wex = np.zeros((NCORES, E, 2, 128, 1), dtype=np.float32)
    for c in range(NCORES):
        for e in range(E):
            ne = L.n_e[e]
            so = L.exp_off[e]
            vals = L.wslot[c][so:so + ne]
            for gi in range(2):
                seg = vals[gi * 128:(gi + 1) * 128]
                wex[c, e, gi, :len(seg), 0] = seg
    in_maps = []
    for c in range(NCORES):
        xc = np.zeros((L.S_tot, D), dtype=np.float32)
        sel = L.perm[c] >= 0
        xc[sel] = xfull[L.perm[c][sel]]
        in_maps.append({
            "xin": xc,
            "wgur": wgu_r, "wdr": wd_r,
            "sgur": sgu_r, "sdr": sd_r,
            "wqr": wq_r, "wkr": wk_r, "wvr": wv_r,
            "bqr": bq_r, "bkr": bk_r, "bvr": bv_r,
            "wslr": wex[c],
            "mkr": L.maskv[c].reshape(L.NB, 128, 1),
            "idnr": ident,
        })

    res = run_bass_kernel_spmd(nc, in_maps, core_ids=list(range(NCORES)))

    out = np.empty((TOKS, D), dtype=np.float32)
    for c in range(NCORES):
        oc = res.results[c]["xout"]
        sel = L.perm[c] >= 0
        out[L.perm[c][sel]] = oc[sel]
    return out.reshape(1, TOKS, D)


# version 09 (final)
# speedup vs baseline: 1.0599x; 1.0599x over previous
import os
import numpy as np
import ml_dtypes
from scipy.special import erfinv

import concourse.bass as bass
import concourse.mybir as mybir
from concourse.bass_utils import run_bass_kernel_spmd

F32 = mybir.dt.float32
BF16 = mybir.dt.bfloat16
AF = mybir.ActivationFunctionType
ALU = mybir.AluOpType
AXX = mybir.AxisListType.X

NCORES = 8
D = 2048
KD = D // 128
I_DIM = 1408
KI = I_DIM // 128
SI = 2816
KSI = SI // 128
E = 16
TOKS = 8192
SCALE = 1.0 / float(np.sqrt(2048.0))


def _threefry2x32(k0, k1, x0, x1):
    M = np.uint64(0xFFFFFFFF)
    def rotl(v, r):
        return ((v << np.uint64(r)) | (v >> np.uint64(32 - r))) & M
    ks0 = np.uint64(k0); ks1 = np.uint64(k1)
    ks2 = (ks0 ^ ks1 ^ np.uint64(0x1BD11BDA)) & M
    x0 = x0.astype(np.uint64); x1 = x1.astype(np.uint64)
    rot = [(13, 15, 26, 6), (17, 29, 16, 24)]
    x0 = (x0 + ks0) & M; x1 = (x1 + ks1) & M
    inject = [(ks1, ks2, 1), (ks2, ks0, 2), (ks0, ks1, 3), (ks1, ks2, 4), (ks2, ks0, 5)]
    for blk in range(5):
        for r in rot[blk % 2]:
            x0 = (x0 + x1) & M
            x1 = rotl(x1, r)
            x1 = x1 ^ x0
        a, b, c = inject[blk]
        x0 = (x0 + a) & M
        x1 = (x1 + b + np.uint64(c)) & M
    return x0.astype(np.uint32), x1.astype(np.uint32)

def _noise(fp64=False):
    n = TOKS * E
    idx = np.arange(n, dtype=np.uint64)
    hi = (idx >> np.uint64(32)).astype(np.uint32)
    lo = (idx & np.uint64(0xFFFFFFFF)).astype(np.uint32)
    b1, b2 = _threefry2x32(0, 42, hi, lo)
    if not fp64:
        bits = b1 ^ b2
        f = ((bits >> np.uint32(9)) | np.uint32(0x3F800000)).view(np.float32) - np.float32(1.0)
        lo32 = np.float32(np.nextafter(np.float32(-1), np.float32(0)))
        u = np.maximum(lo32, (f * (np.float32(1.0) - lo32) + lo32).astype(np.float32))
        nrm = (np.float32(np.sqrt(2.0)) * erfinv(u.astype(np.float64))).astype(np.float32)
    else:
        bits = (b1.astype(np.uint64) << np.uint64(32)) | b2.astype(np.uint64)
        f = ((bits >> np.uint64(12)) | np.uint64(0x3FF0000000000000)).view(np.float64) - 1.0
        lo64 = np.nextafter(-1.0, 0.0)
        u = np.maximum(lo64, f * (1.0 - lo64) + lo64)
        nrm = (np.sqrt(2.0) * erfinv(u)).astype(np.float32)
    return nrm.reshape(TOKS, E)

def _routing(fp64=False):
    noise = _noise(fp64)
    s = (1.0 / (1.0 + np.exp(-noise.astype(np.float64)))).astype(np.float32)
    ids = np.argsort(-s.astype(np.float64), axis=-1, kind="stable")[:, :2]
    w = np.take_along_axis(s, ids, axis=-1).astype(np.float32)
    w = (w / (w.sum(-1, keepdims=True) + np.float32(1e-8))).astype(np.float32)
    return ids.astype(np.int64), w


class Layout:
    pass

def _build_layout(ids, w):
    L = Layout()
    a = np.minimum(ids[:, 0], ids[:, 1])
    b = np.maximum(ids[:, 0], ids[:, 1])
    pk = a * E + b
    tok_by_pair = {}
    for t in np.argsort(pk, kind="stable"):
        tok_by_pair.setdefault(int(pk[t]), []).append(int(t))
    pairs = []
    for key in sorted(tok_by_pair):
        toks = tok_by_pair[key]
        q = (len(toks) + NCORES - 1) // NCORES
        pairs.append((key // E, key % E, q, toks))
    L.pairs = pairs
    S = sum(p[2] for p in pairs)
    L.S = S
    L.S_tot = ((S + 127) // 128) * 128
    L.NB = L.S_tot // 128
    off = 0
    L.pair_off = []
    for (_, _, q, _) in pairs:
        L.pair_off.append(off)
        off += q
    L.exp_runs = []
    L.exp_off = []
    L.n_e = []
    soff = 0
    for e in range(E):
        runs = []
        ne = 0
        for (pa, pb_, q, _), poff in zip(pairs, L.pair_off):
            if pa == e or pb_ == e:
                runs.append((ne, poff, q))
                ne += q
        L.exp_runs.append(runs)
        L.exp_off.append(soff)
        L.n_e.append(ne)
        soff += ne
    L.Ltot = soff
    L.LB = (soff + 127) // 128
    L.perm = np.full((NCORES, L.S_tot), -1, dtype=np.int64)
    L.maskv = np.zeros((NCORES, L.S_tot), dtype=np.float32)
    for (pa, pb_, q, toks), poff in zip(pairs, L.pair_off):
        m = len(toks)
        base, rem = divmod(m, NCORES)
        start = 0
        for c in range(NCORES):
            cnt = base + (1 if c < rem else 0)
            for i in range(cnt):
                L.perm[c, poff + i] = toks[start + i]
                L.maskv[c, poff + i] = 1.0
            start += cnt
    L.wslot = np.zeros((NCORES, L.LB * 128), dtype=np.float32)
    for e in range(E):
        so = L.exp_off[e]
        for (ro, poff, q) in L.exp_runs[e]:
            for i in range(q):
                for c in range(NCORES):
                    t = L.perm[c, poff + i]
                    if t >= 0:
                        if ids[t, 0] == e:
                            L.wslot[c, so + ro + i] = w[t, 0]
                        else:
                            L.wslot[c, so + ro + i] = w[t, 1]
    return L


def _segments(dst0, src0, length, blk=128):
    out = []
    done = 0
    while done < length:
        d = dst0 + done
        s = src0 + done
        step = min(length - done, blk - d % blk, blk - s % blk)
        out.append((d // blk, d % blk, s // blk, s % blk, step))
        done += step
    return out



def build_program(L):
    nc = bass.Bass("TRN2", target_bir_lowering=False)
    S_tot, NB, LB = L.S_tot, L.NB, L.LB
    NF = (S_tot + 511) // 512
    ftiles = [(f * 512, min(512, S_tot - f * 512)) for f in range(NF)]
    EXW = ((max(L.n_e[2 * o] + L.n_e[2 * o + 1] for o in range(E // 2)) + 127) // 128) * 128
    NEMAX = max(L.n_e)

    xin = nc.dram_tensor("xin", [S_tot, D], F32, kind="ExternalInput").ap()
    wgr = nc.dram_tensor("wgr", [E, KI, 128, D], F32, kind="ExternalInput").ap()
    wur = nc.dram_tensor("wur", [E, KI, 128, D], F32, kind="ExternalInput").ap()
    wdr = nc.dram_tensor("wdr", [E, KI, 128, D], F32, kind="ExternalInput").ap()
    sgr = nc.dram_tensor("sgr", [KSI, 128, D], F32, kind="ExternalInput").ap()
    sur = nc.dram_tensor("sur", [KSI, 128, D], F32, kind="ExternalInput").ap()
    sdr = nc.dram_tensor("sdr", [KSI, 128, D], F32, kind="ExternalInput").ap()
    wqr = nc.dram_tensor("wqr", [KD, 128, D], F32, kind="ExternalInput").ap()
    wkr = nc.dram_tensor("wkr", [KD, 128, D], F32, kind="ExternalInput").ap()
    wvr = nc.dram_tensor("wvr", [KD, 128, D], F32, kind="ExternalInput").ap()
    bqr = nc.dram_tensor("bqr", [KD, 128, 1], F32, kind="ExternalInput").ap()
    bkr = nc.dram_tensor("bkr", [KD, 128, 1], F32, kind="ExternalInput").ap()
    bvr = nc.dram_tensor("bvr", [KD, 128, 1], F32, kind="ExternalInput").ap()
    wslr = nc.dram_tensor("wslr", [E, 2, 128, 1], F32, kind="ExternalInput").ap()
    mkr = nc.dram_tensor("mkr", [NB, 128, 1], F32, kind="ExternalInput").ap()
    idnr = nc.dram_tensor("idnr", [128, 128], BF16, kind="ExternalInput").ap()
    xout = nc.dram_tensor("xout", [S_tot, D], F32, kind="ExternalOutput").ap()

    qscr = nc.dram_tensor("qscr", [KD, NB, 128, 128], BF16)
    vscr = nc.dram_tensor("vscr", [KD, 128, S_tot], BF16)
    cc_s_in = nc.dram_tensor("cc_s_in", [D, D], BF16)
    cc_s_out = nc.dram_tensor("cc_s_out", [D // NCORES, D], BF16)
    cc_p_in = nc.dram_tensor("cc_p_in", [KD, 128, D // NCORES], BF16)
    cc_p_out = nc.dram_tensor("cc_p_out", [NCORES, KD, 128, D // NCORES], BF16)
    RG = [list(range(NCORES))]

    import contextlib
    stck = contextlib.ExitStack()
    def sbuf(name, shape, dt):
        return stck.enter_context(nc.sbuf_tensor(name, shape, dt))
    def psum(name, shape, dt=F32):
        return stck.enter_context(nc.psum_tensor(name, shape, dt))

    x = [sbuf(f"x{b}", [128, D], F32) for b in range(NB)]
    xn = [sbuf(f"xn{k}", [128, S_tot], BF16) for k in range(KD)]
    tm9 = [sbuf(f"tm{b}", [128, D], BF16) for b in range(NB)]
    dtm = [sbuf(f"dtm{j}", [128, D], BF16) for j in range(2)]
    NSH = 1
    hsh = [sbuf(f"hsh{j}", [128, S_tot], BF16) for j in range(NSH)]
    ex = [sbuf(f"ex{k}", [128, EXW], BF16) for k in range(KD)]
    he = [sbuf(f"he{k}", [128, NEMAX], BF16) for k in range(KI)]
    wA = [sbuf(f"wA{j}", [128, D], BF16) for j in range(2)]
    wB = [sbuf(f"wB{j}", [128, D], BF16) for j in range(2)]
    wC = [sbuf(f"wC{j}", [128, D], BF16) for j in range(2)]
    pex = sbuf("pex", [128, D], F32)
    sctm = sbuf("sctm", [128, D], BF16)
    wcol = [sbuf(f"wc{g}", [128, 1], F32) for g in range(2)]
    mcol = [sbuf(f"mc{b}", [128, 1], F32) for b in range(NB)]
    bqc = [sbuf(f"bqc{k}", [128, 1], F32) for k in range(KD)]
    bkc = [sbuf(f"bkc{k}", [128, 1], F32) for k in range(KD)]
    bvc = [sbuf(f"bvc{k}", [128, 1], F32) for k in range(KD)]
    idt = sbuf("idt", [128, 128], BF16)
    st6 = sbuf("st6", [128, 24], F32)
    red4 = sbuf("red4", [128, 4], F32)
    mv2 = sbuf("mv2", [128, 2], F32)
    rstd = sbuf("rstd", [128, 1], F32)
    nmr = sbuf("nmr", [128, 1], F32)
    epsb = sbuf("epsb", [128, 1], F32)
    mx1 = sbuf("mx1", [128, 1], F32)
    sm1 = sbuf("sm1", [128, 1], F32)

    pb = [psum(f"pb{i}", [128, 512]) for i in range(8)]

    sems, ctr, lastw = {}, {}, {}
    for name in ("sq", "sg2", "smm", "sve", "sse", "scc", "sgm"):
        sems[name] = stck.enter_context(nc.semaphore(name))
        ctr[name] = 0

    TEN, VEC, SCA, GPS, SYN = nc.tensor, nc.vector, nc.scalar, nc.gpsimd, nc.sync
    ENGSEM = {id(TEN): "smm", id(VEC): "sve", id(SCA): "sse"}

    def inc(instr, name, amt):
        instr.then_inc(sems[name], amt)
        ctr[name] += amt
        return (name, ctr[name])

    def need(eng, *tokens):
        for t in tokens:
            if t is None:
                continue
            name, val = t
            key = (id(eng), name)
            if lastw.get(key, -1) >= val:
                continue
            eng.wait_ge(sems[name], val)
            lastw[key] = val

    def dma(eng, qsem, out, in_, accum=False, deps=()):
        need(eng, *deps)
        if accum:
            ins = eng.dma_start(out=out, in_=in_, accum_op=ALU.add)
        else:
            ins = eng.dma_start(out=out, in_=in_)
        return inc(ins, qsem, 16)

    def op(eng, method, *args, deps=(), **kw):
        need(eng, *deps)
        ins = getattr(eng, method)(*args, **kw)
        return inc(ins, ENGSEM[id(eng)], 1)

    bank_free = {i: None for i in range(8)}
    buf_free = {}

    def mm(bi, out_ap, lhsT, rhs, start, stop, deps=()):
        d = deps + ((bank_free[bi],) if start else ())
        return op(TEN, "matmul", out_ap, lhsT, rhs, start=start, stop=stop, deps=d)

    def transpose128(bi, src_ap, deps=()):
        return mm(bi, pb[bi][:, 0:128], src_ap, idt[:, :], True, True,
                  deps=deps + (tok_id,))

    tok_x = [dma(SYN, "sq", x[b][:, :], xin[b * 128:(b + 1) * 128, :]) for b in range(NB)]

    tok_mc = [dma(SYN, "sq", mcol[b][:, :], mkr[b]) for b in range(NB)]
    tok_bq = [dma(SYN, "sq", bqc[k][:, :], bqr[k]) for k in range(KD)]
    tok_bk = [dma(SYN, "sq", bkc[k][:, :], bkr[k]) for k in range(KD)]
    tok_bv = [dma(SYN, "sq", bvc[k][:, :], bvr[k]) for k in range(KD)]
    tok_id = dma(SYN, "sq", idt[:, :], idnr[:, :])
    tok_eps = inc(GPS.memset(epsb[:, :], 1e-5), "sgm", 1)

    state = {"xtok": list(tok_x)}

    def ln_to_fm(masked, keep_tm):
        toks = []
        for b in range(NB):
            t1 = None
            for c4 in range(4):
                t1 = op(VEC, "bn_stats", st6[:, c4 * 6:(c4 + 1) * 6],
                        x[b][:, c4 * 512:(c4 + 1) * 512], deps=(state["xtok"][b],))
            t2 = op(VEC, "bn_aggr", mv2[:, :], st6[:, :], deps=(t1,))
            t3 = op(SCA, "activation", rstd[:, :], mv2[:, 1:2], AF.Sqrt, bias=epsb[:, :],
                    deps=(t2, tok_eps))
            t4 = op(VEC, "reciprocal", rstd[:, :], rstd[:, :], deps=(t3,))
            if masked:
                t4 = op(VEC, "tensor_mul", rstd[:, :], rstd[:, :], mcol[b][:, :],
                        deps=(t4, tok_mc[b]))
            t5 = None
            for c4 in range(4):
                sl = slice(c4 * 512, (c4 + 1) * 512)
                t5 = op(VEC, "tensor_scalar", sctm[:, sl], x[b][:, sl], mv2[:, 0:1],
                        rstd[:, :], ALU.subtract, ALU.mult, deps=(t4,) + tuple(
                            t for t in (buf_free.get("sctm"),) if t))
                if keep_tm:
                    toks.append(op(VEC, "tensor_scalar", tm9[b][:, sl], x[b][:, sl],
                                   mv2[:, 0:1], rstd[:, :], ALU.subtract, ALU.mult,
                                   deps=(t4,)))
            for k in range(KD):
                bi = 6 + (k % 2)
                tt = transpose128(bi, sctm[:, k * 128:(k + 1) * 128], deps=(t5,))
                tc = op(VEC, "tensor_copy", xn[k][:, b * 128:(b + 1) * 128],
                        pb[bi][:, 0:128], deps=(tt,))
                bank_free[bi] = tc
                toks.append(tc)
            buf_free["sctm"] = toks[-1]
        return toks[-1]

    def ffn_pass():
        xn_ready = ln_to_fm(masked=False, keep_tm=True)
        tm_ready = xn_ready

        last_sh_add = None
        for g0 in range(0, KSI, NSH):
            grp = list(range(g0, min(g0 + NSH, KSI)))
            h_last = None
            gu_tiles = []
            for j, si in enumerate(grp):
                pA, pB = wA[j % 2], wB[j % 2]
                depA = tuple(t for t in (buf_free.get(f"wA{j % 2}"),) if t)
                depB = tuple(t for t in (buf_free.get(f"wB{j % 2}"),) if t)
                tA = dma(GPS, "sg2", pA[:, :], sgr[si], deps=depA)
                tB = dma(GPS, "sg2", pB[:, :], sur[si], deps=depB)
                for (f0, flen) in ftiles:
                    g_mm = None
                    for k in range(KD):
                        g_mm = mm(0, pb[0][:, :flen], pA[:, k * 128:(k + 1) * 128],
                                  xn[k][:, f0:f0 + flen], k == 0, k == KD - 1,
                                  deps=(tA, xn_ready))
                    u_mm = None
                    for k in range(KD):
                        u_mm = mm(1, pb[1][:, :flen], pB[:, k * 128:(k + 1) * 128],
                                  xn[k][:, f0:f0 + flen], k == 0, k == KD - 1, deps=(tB,))
                    ts = op(SCA, "activation", pex[:, 0:flen], pb[0][:, :flen], AF.Silu,
                            deps=(g_mm,) + tuple(t for t in (buf_free.get("sil"),) if t))
                    th = op(VEC, "tensor_mul", hsh[j][:, f0:f0 + flen], pex[:, 0:flen],
                            pb[1][:, :flen], deps=(ts, u_mm))
                    buf_free["sil"] = th
                    bank_free[0] = ts
                    bank_free[1] = th
                    h_last = th
                buf_free[f"wA{j % 2}"] = h_last
                buf_free[f"wB{j % 2}"] = h_last
            sd_tiles = []
            for j, si in enumerate(grp):
                buf = wC[j % 2]
                td = dma(GPS, "sg2", buf[:, :], sdr[si],
                         deps=tuple(t for t in (buf_free.get(f"wC{j % 2}"),) if t))
                th = None
                for c4 in range(4):
                    th = op(SCA, "mul", buf[:, c4 * 512:(c4 + 1) * 512],
                            buf[:, c4 * 512:(c4 + 1) * 512], 0.5, deps=(td,))
                sd_tiles.append((buf, th))
            dn_last = None
            for b in range(NB):
                for dq in range(4):
                    bi = 2 + dq
                    d_mm = None
                    for j in range(len(grp)):
                        buf, th = sd_tiles[j]
                        d_mm = mm(bi, pb[bi][:, :], hsh[j][:, b * 128:(b + 1) * 128],
                                  buf[:, dq * 512:(dq + 1) * 512], j == 0, j == len(grp) - 1,
                                  deps=(th, h_last))
                    ta = op(VEC, "tensor_add", x[b][:, dq * 512:(dq + 1) * 512],
                            x[b][:, dq * 512:(dq + 1) * 512], pb[bi][:, :],
                            deps=(d_mm, state["xtok"][b]))
                    bank_free[bi] = ta
                    state["xtok"][b] = ta
                    dn_last = ta
            buf_free["wC0"] = dn_last
            buf_free["wC1"] = dn_last
            last_sh_add = dn_last

        prev_scat = None
        for oc in range(E // 2):
            e0, e1 = 2 * oc, 2 * oc + 1
            oct_off = L.exp_off[e0]
            oct_len = L.n_e[e0] + L.n_e[e1]
            oct_blocks = (oct_len + 127) // 128
            pk_last = None
            for db in range(oct_blocks):
                rows = min(128, oct_len - db * 128)
                segs = []
                mz = None
                if rows < 128:
                    mz = op(VEC, "memset", sctm[:, :], 0.0,
                            deps=tuple(t for t in (buf_free.get("sctm"),) if t))
                    segs.append(mz)
                for e in (e0, e1):
                    lo = L.exp_off[e] - oct_off
                    for (ro, poff, q) in L.exp_runs[e]:
                        for (d_b, d_r, s_b, s_r, ln_) in _segments(lo + ro, poff, q):
                            if d_b != db:
                                continue
                            segs.append(dma(GPS, "sg2", sctm[d_r:d_r + ln_, :],
                                            tm9[s_b][s_r:s_r + ln_, :],
                                            deps=(tm_ready,) + tuple(
                                                t for t in (mz, buf_free.get("sctm")) if t)))
                tpk = None
                for k in range(KD):
                    bi = 6 + (k % 2)
                    tt = transpose128(bi, sctm[:, k * 128:(k + 1) * 128], deps=tuple(segs))
                    tpk = op(VEC, "tensor_copy", ex[k][:, db * 128:(db + 1) * 128],
                             pb[bi][:, 0:128], deps=(tt,) + tuple(
                                 t for t in (buf_free.get("ex"),) if t))
                    bank_free[bi] = tpk
                buf_free["sctm"] = tpk
                pk_last = tpk
            for e in (e0, e1):
                lo = L.exp_off[e] - oct_off
                ne = L.n_e[e]
                h_tok = []
                for i in range(KI):
                    pA, pB = wA[i % 2], wB[i % 2]
                    depA = (h_tok[i - 2],) if i >= 2 else tuple(
                        t for t in (buf_free.get(f"wA{i % 2}"),) if t)
                    depB = (h_tok[i - 2],) if i >= 2 else tuple(
                        t for t in (buf_free.get(f"wB{i % 2}"),) if t)
                    tA = dma(GPS, "sg2", pA[:, :], wgr[e, i], deps=depA)
                    tB = dma(GPS, "sg2", pB[:, :], wur[e, i], deps=depB)
                    g_mm = None
                    for k in range(KD):
                        g_mm = mm(0, pb[0][:, :ne], pA[:, k * 128:(k + 1) * 128],
                                  ex[k][:, lo:lo + ne], k == 0, k == KD - 1,
                                  deps=(tA, pk_last))
                    u_mm = None
                    for k in range(KD):
                        u_mm = mm(1, pb[1][:, :ne], pB[:, k * 128:(k + 1) * 128],
                                  ex[k][:, lo:lo + ne], k == 0, k == KD - 1, deps=(tB,))
                    ts = op(SCA, "activation", pex[:, 0:ne], pb[0][:, :ne], AF.Silu,
                            deps=(g_mm,) + tuple(t for t in (buf_free.get("sil"),) if t))
                    th = op(VEC, "tensor_mul", he[i][:, :ne], pex[:, 0:ne], pb[1][:, :ne],
                            deps=(ts, u_mm))
                    buf_free["sil"] = th
                    bank_free[0] = ts
                    bank_free[1] = th
                    h_tok.append(th)
                buf_free["wA0"] = h_tok[-1]; buf_free["wA1"] = h_tok[-1]
                buf_free["wB0"] = h_tok[-1]; buf_free["wB1"] = h_tok[-1]
                grp_rows = [(g * 128, min(128, ne - g * 128))
                            for g in range((ne + 127) // 128)]
                last_mm = [None] * 8
                dn_t = []
                tile_last = {}
                for i in range(KI):
                    buf = wC[i % 2]
                    depW = (tile_last[i - 2],) if i >= 2 else tuple(
                        t for t in (buf_free.get(f"wC{i % 2}"),) if t)
                    dn_t.append(dma(GPS, "sg2", buf[:, :], wdr[e, i], deps=depW))
                    for gi, (g0, rows) in enumerate(grp_rows):
                        for dq in range(4):
                            bi = gi * 4 + dq
                            last_mm[bi] = mm(bi, pb[bi][:rows, :],
                                             he[i][:, g0:g0 + rows],
                                             buf[:, dq * 512:(dq + 1) * 512],
                                             i == 0, i == KI - 1,
                                             deps=(dn_t[i], h_tok[i]))
                            tile_last[i] = last_mm[bi]
                buf_free["wC0"] = tile_last[KI - 1]; buf_free["wC1"] = tile_last[KI - 1]
                ev = []
                w_toks = []
                for gi in range(len(grp_rows)):
                    tw = dma(SYN, "sq", wcol[gi][:, :], wslr[e, gi],
                             deps=tuple(t for t in (buf_free.get(f"wcol{gi}"),) if t))
                    tw = op(SCA, "mul", wcol[gi][:, :], wcol[gi][:, :], 0.5, deps=(tw,))
                    w_toks.append(tw)
                for gi, (g0, rows) in enumerate(grp_rows):
                    for dq in range(4):
                        bi = gi * 4 + dq
                        tv = op(VEC, "tensor_scalar_mul",
                                dtm[gi][0:rows, dq * 512:(dq + 1) * 512],
                                pb[bi][0:rows, :], wcol[gi][0:rows, :],
                                deps=(last_mm[bi], w_toks[gi]) + tuple(
                                    t for t in (buf_free.get(f"dtm{gi}"),) if t))
                        ev.append(tv)
                        bank_free[bi] = tv
                    buf_free[f"wcol{gi}"] = ev[-1]
                ev_tok = ev[-1]
                sc_deps = (ev_tok, last_sh_add)
                if prev_scat is not None:
                    sc_deps = sc_deps + (prev_scat,)
                for gi, (g0, rows) in enumerate(grp_rows):
                    pass
                for (ro, poff, q) in L.exp_runs[e]:
                    done = 0
                    while done < q:
                        sl = lo * 0 + ro + done
                        gi, sr = sl // 128, sl % 128
                        dtok = poff + done
                        db, dr = dtok // 128, dtok % 128
                        step = min(q - done, 128 - sr, 128 - dr)
                        td = dma(GPS, "sg2", x[db][dr:dr + step, :],
                                 dtm[gi][sr:sr + step, :], accum=True, deps=sc_deps)
                        state["xtok"][db] = td
                        done += step
                prev_scat = ("sg2", ctr["sg2"])
                buf_free["dtm0"] = prev_scat
                buf_free["dtm1"] = prev_scat
        buf_free["ex"] = prev_scat

    def attention():
        xn_ready = ln_to_fm(masked=True, keep_tm=False)

        def proj(wsrc, bias_cols, bias_toks, sink):
            for mc in range(KD):
                buf = wA[mc % 2]
                depW = tuple(t for t in (buf_free.get(f"wA{mc % 2}"),) if t)
                tW = dma(GPS, "sg2", buf[:, :], wsrc[mc], deps=depW)
                act = None
                for (f0, flen) in ftiles:
                    p_mm = None
                    for k in range(KD):
                        p_mm = mm(0, pb[0][:, :flen], buf[:, k * 128:(k + 1) * 128],
                                  xn[k][:, f0:f0 + flen], k == 0, k == KD - 1,
                                  deps=(tW, xn_ready))
                    act = op(VEC, "tensor_scalar_add", sctm[:, f0:f0 + flen],
                             pb[0][:, :flen], bias_cols[mc][:, :],
                             deps=(p_mm, bias_toks[mc]) + tuple(
                                 t for t in (buf_free.get("sctm"),) if t))
                    bank_free[0] = act
                last = sink(mc, act)
                buf_free["sctm"] = last
                buf_free[f"wA{mc % 2}"] = last

        def q_sink(mc, t_act):
            out = None
            for b in range(NB):
                bi = 6 + (b % 2)
                tt = transpose128(bi, sctm[:, b * 128:(b + 1) * 128], deps=(t_act,))
                tmk = op(VEC, "tensor_scalar_mul", he[0][:, 0:128], pb[bi][:, 0:128],
                         mcol[b][:, :], deps=(tt, tok_mc[b]) + tuple(
                             t for t in (buf_free.get("spb"),) if t))
                bank_free[bi] = tmk
                out = dma(SYN, "sq", qscr[mc, b], he[0][:, 0:128], deps=(tmk,))
                buf_free["spb"] = out
            return out
        proj(wqr, bqc, tok_bq, q_sink)

        def k_sink(mc, t_act):
            out = None
            for b in range(NB):
                bi = 6 + (b % 2)
                tt = transpose128(bi, sctm[:, b * 128:(b + 1) * 128], deps=(t_act,))
                out = op(VEC, "tensor_scalar_mul", tm9[b][:, mc * 128:(mc + 1) * 128],
                         pb[bi][:, 0:128], mcol[b][:, :], deps=(tt, tok_mc[b]))
                bank_free[bi] = out
            return out
        proj(wkr, bkc, tok_bk, k_sink)

        def v_sink(mc, t_act):
            return dma(SYN, "sq", vscr[mc], sctm[:, 0:S_tot], deps=(t_act,))
        proj(wvr, bvc, tok_bv, v_sink)

        k_done = ("sve", ctr["sve"])
        vq_done = ("sq", ctr["sq"])

        ex_mm = {}
        stgs = [dtm[0], dtm[1]]
        for lc in range(KD):
            par = lc % 2
            q_toks = []
            for b in range(NB):
                q_toks.append(dma(GPS, "sg2", ex[b][:, par * 128:par * 128 + 128],
                                  qscr[lc, b],
                                  deps=((ex_mm.get(par),) if ex_mm.get(par) else ())))
            sbuf_st = stgs[par]
            ev_t = None
            for mq in range(4):
                bi = 2 + mq
                s_mm = None
                for b in range(NB):
                    s_mm = mm(bi, pb[bi][:, :], ex[b][:, par * 128:par * 128 + 128],
                              tm9[b][:, mq * 512:(mq + 1) * 512], b == 0, b == NB - 1,
                              deps=(q_toks[b], k_done))
                ev_t = op(VEC, "tensor_copy", sbuf_st[:, mq * 512:(mq + 1) * 512],
                          pb[bi][:, :],
                          deps=(s_mm,) + tuple(
                              t for t in (buf_free.get(f"dtm{par}"),) if t))
                bank_free[bi] = ev_t
            ex_mm[par] = ("smm", ctr["smm"])
            tsd = dma(SYN, "sq", cc_s_in[lc * 128:(lc + 1) * 128, :], sbuf_st[:, :],
                      deps=(ev_t,))
            buf_free[f"dtm{par}"] = tsd
        sc_done = ("sq", ctr["sq"])

        need(GPS, sc_done)
        tcc = inc(GPS.collective_compute("ReduceScatter", ALU.add, replica_groups=RG,
                                         ins=[cc_s_in.ap().opt()],
                                         outs=[cc_s_out.ap().opt()]), "scc", 1)

        ag_last = None
        for sb in range(2):
            tl = dma(GPS, "sg2", pex[:, :], cc_s_out[sb * 128:(sb + 1) * 128, :],
                     deps=(tcc,) + tuple(t for t in (buf_free.get("pex"),) if t))
            tm_ = None
            for c4 in range(4):
                tm_ = op(VEC, "reduce_max", red4[:, c4:c4 + 1],
                         pex[:, c4 * 512:(c4 + 1) * 512], axis=AXX, deps=(tl,))
            tm_ = op(VEC, "reduce_max", mx1[:, :], red4[:, :], axis=AXX, deps=(tm_,))
            tn = op(SCA, "mul", nmr[:, :], mx1[:, :], -SCALE, deps=(tm_,))
            te = None
            for c4 in range(4):
                sl = slice(c4 * 512, (c4 + 1) * 512)
                te = op(SCA, "activation", pex[:, sl], pex[:, sl], AF.Exp, bias=nmr[:, :],
                        scale=SCALE, deps=(tn,))
            tsum = None
            for c4 in range(4):
                tsum = op(VEC, "reduce_sum", red4[:, c4:c4 + 1],
                          pex[:, c4 * 512:(c4 + 1) * 512], axis=AXX, deps=(te,))
            tsum = op(VEC, "reduce_sum", sm1[:, :], red4[:, :], axis=AXX, deps=(tsum,))
            tre = op(VEC, "reciprocal", sm1[:, :], sm1[:, :], deps=(tsum,))
            tp = None
            for c4 in range(4):
                sl = slice(c4 * 512, (c4 + 1) * 512)
                tp = op(VEC, "tensor_scalar_mul", sctm[:, sl], pex[:, sl], sm1[:, :],
                        deps=(tre,) + tuple(t for t in (buf_free.get("sctm"),) if t))
            buf_free["pex"] = tp
            for mc in range(KD):
                bi = 6 + (mc % 2)
                tt = transpose128(bi, sctm[:, mc * 128:(mc + 1) * 128], deps=(tp,))
                tc = op(VEC, "tensor_copy", he[0][:, 0:128], pb[bi][:, 0:128],
                        deps=(tt,) + tuple(t for t in (buf_free.get("spb"),) if t))
                bank_free[bi] = tc
                ag_last = dma(SYN, "sq", cc_p_in[mc, :, sb * 128:(sb + 1) * 128], he[0][:, 0:128],
                              deps=(tc,))
                buf_free["spb"] = ag_last
            buf_free["sctm"] = ag_last
        need(GPS, ("sq", ctr["sq"]))
        tag = inc(GPS.collective_compute("AllGather", ALU.bypass, replica_groups=RG,
                                         ins=[cc_p_in.ap().opt()],
                                         outs=[cc_p_out.ap().opt()]), "scc", 1)

        ex_mm2 = {}
        wb_user = {0: buf_free.get("wB0"), 1: buf_free.get("wB1")}
        for cb in range(NB):
            par = cb % 2
            v_toks = []
            for k in range(KD):
                v_toks.append(dma(GPS, "sg2", ex[k][:, par * 128:par * 128 + 128],
                                  vscr[k, :, cb * 128:(cb + 1) * 128],
                                  deps=((ex_mm2.get(par),) if ex_mm2.get(par) else ())))
            mm_lq = {}
            for k in range(KD):
                buf = wB[k % 2]
                depB = (wb_user.get(k % 2),) if wb_user.get(k % 2) else ()
                pt_t = None
                for r in range(NCORES):
                    pt_t = dma(GPS, "sg2", buf[:, r * 256:(r + 1) * 256], cc_p_out[r, k],
                               deps=(tag,) + (depB if r == 0 else ()))
                for lq in range(4):
                    bi = 2 + lq
                    mm_lq[lq] = mm(bi, pb[bi][:, :], ex[k][:, par * 128:par * 128 + 128],
                                   buf[:, lq * 512:(lq + 1) * 512], k == 0, k == KD - 1,
                                   deps=(v_toks[k], pt_t))
                wb_user[k % 2] = mm_lq[3]
            ex_mm2[par] = ("smm", ctr["smm"])
            for lq in range(4):
                bi = 2 + lq
                ta = op(VEC, "tensor_add", x[cb][:, lq * 512:(lq + 1) * 512],
                        x[cb][:, lq * 512:(lq + 1) * 512], pb[bi][:, :],
                        deps=(mm_lq[lq], state["xtok"][cb]))
                bank_free[bi] = ta
                state["xtok"][cb] = ta
            buf_free["wB0"] = wb_user[0]
            buf_free["wB1"] = wb_user[1]

    ffn_pass()
    attention()
    ffn_pass()

    for b in range(NB):
        dma(SYN, "sq", xout[b * 128:(b + 1) * 128, :], x[b][:, :],
            deps=(state["xtok"][b],))
    need(GPS, ("sq", ctr["sq"]))

    stck.close()
    return nc



def kernel(L0_attn_x, L0_attn_Wq, L0_attn_bq, L0_attn_Wk, L0_attn_bk, L0_attn_Wv,
           L0_attn_bv, L0_ffn_router_w, L0_ffn_router_b, L0_ffn_w_gate, L0_ffn_w_up,
           L0_ffn_w_down, L0_ffn_shared_gate, L0_ffn_shared_up, L0_ffn_shared_down):
    fp64 = os.environ.get("BASS_SIM_FP64", "") not in ("", "0")
    ids, w = _routing(fp64)
    L = _build_layout(ids, w)
    nc = build_program(L)

    xfull = np.ascontiguousarray(np.asarray(L0_attn_x, dtype=np.float32).reshape(TOKS, D))

    def relay_lhs(m):
        Dm, N = m.shape
        r = m.reshape(Dm // 128, 128, N // 128, 128).transpose(2, 1, 0, 3)
        return np.ascontiguousarray(r.reshape(N // 128, 128, Dm))

    wq_r = relay_lhs(np.asarray(L0_attn_Wq, np.float32).T)
    wk_r = relay_lhs(np.asarray(L0_attn_Wk, np.float32).T)
    wv_r = relay_lhs(np.asarray(L0_attn_Wv, np.float32).T)
    wg_r = np.stack([relay_lhs(np.asarray(L0_ffn_w_gate[e], np.float32)) for e in range(E)])
    wu_r = np.stack([relay_lhs(np.asarray(L0_ffn_w_up[e], np.float32)) for e in range(E)])
    wd_r = np.stack([np.ascontiguousarray(np.asarray(L0_ffn_w_down[e], np.float32)
                                          .reshape(KI, 128, D)) for e in range(E)])
    sg_r = relay_lhs(np.asarray(L0_ffn_shared_gate, np.float32))
    su_r = relay_lhs(np.asarray(L0_ffn_shared_up, np.float32))
    sd_r = np.ascontiguousarray(np.asarray(L0_ffn_shared_down, np.float32).reshape(KSI, 128, D))
    bq_r = np.asarray(L0_attn_bq, np.float32).reshape(KD, 128, 1)
    bk_r = np.asarray(L0_attn_bk, np.float32).reshape(KD, 128, 1)
    bv_r = np.asarray(L0_attn_bv, np.float32).reshape(KD, 128, 1)
    ident = np.eye(128, dtype=ml_dtypes.bfloat16)

    wex = np.zeros((NCORES, E, 2, 128, 1), dtype=np.float32)
    for c in range(NCORES):
        for e in range(E):
            ne = L.n_e[e]
            so = L.exp_off[e]
            vals = L.wslot[c][so:so + ne]
            for gi in range(2):
                seg = vals[gi * 128:(gi + 1) * 128]
                wex[c, e, gi, :len(seg), 0] = seg
    in_maps = []
    for c in range(NCORES):
        xc = np.zeros((L.S_tot, D), dtype=np.float32)
        sel = L.perm[c] >= 0
        xc[sel] = xfull[L.perm[c][sel]]
        in_maps.append({
            "xin": xc,
            "wgr": wg_r, "wur": wu_r, "wdr": wd_r,
            "sgr": sg_r, "sur": su_r, "sdr": sd_r,
            "wqr": wq_r, "wkr": wk_r, "wvr": wv_r,
            "bqr": bq_r, "bkr": bk_r, "bvr": bv_r,
            "wslr": wex[c],
            "mkr": L.maskv[c].reshape(L.NB, 128, 1),
            "idnr": ident,
        })

    res = run_bass_kernel_spmd(nc, in_maps, core_ids=list(range(NCORES)))

    out = np.empty((TOKS, D), dtype=np.float32)
    for c in range(NCORES):
        oc = res.results[c]["xout"]
        sel = L.perm[c] >= 0
        out[L.perm[c][sel]] = oc[sel]
    return out.reshape(1, TOKS, D)
